# version 09
# baseline (speedup 1.0000x reference)
import numpy as np
import contextlib
import concourse.bass as bass
import concourse.mybir as mybir
from concourse.bass_utils import run_bass_kernel_spmd
from concourse.aot_env import allow_ici_all_to_all

F32 = mybir.dt.float32
BF16 = mybir.dt.bfloat16
I32 = mybir.dt.int32
AF = mybir.ActivationFunctionType
ALU = mybir.AluOpType

B = 2
S = 8192
D = 1024
NST = 16
RR = 64
HID = 3072
P = 128
TQ = 2048
TW = 1024
NC = 8
NSEG = 4
SGT = 2048
CT = 256
EPS = 1e-5
LN127 = float(np.log(127.0))


class Sem:
    def __init__(self, h):
        self.h = h
        self.n = 0

    def dma(self):
        self.n += 16
        return self.n

    def inc(self):
        self.n += 1
        return self.n


def build_program():
    nc = bass.Bass("TRN2", target_bir_lowering=False, detect_race_conditions=False)

    xa = nc.dram_tensor("xa", [D, TQ], F32, kind="ExternalInput").ap()
    xb = nc.dram_tensor("xb", [B, P, S], F32, kind="ExternalInput").ap()
    xc = nc.dram_tensor("xc", [B, D, TW], F32, kind="ExternalInput").ap()
    wcat = nc.dram_tensor("wcat", [D, 96], F32, kind="ExternalInput").ap()
    wdt = nc.dram_tensor("wdt", [RR, P], F32, kind="ExternalInput").ap()
    bdt = nc.dram_tensor("bdt", [P, 1], F32, kind="ExternalInput").ap()
    alog = nc.dram_tensor("alog", [P, NST], F32, kind="ExternalInput").ap()
    dpv = nc.dram_tensor("dpv", [P, 1], F32, kind="ExternalInput").ap()
    lng = nc.dram_tensor("lng", [P, 8], F32, kind="ExternalInput").ap()
    lnb = nc.dram_tensor("lnb", [P, 8], F32, kind="ExternalInput").ap()
    wout = nc.dram_tensor("wout", [D, D], F32, kind="ExternalInput").ap()
    wg = nc.dram_tensor("wg", [D, HID], I32, kind="ExternalInput").ap()
    wu = nc.dram_tensor("wu", [D, HID], I32, kind="ExternalInput").ap()
    wd = nc.dram_tensor("wd", [HID, D], I32, kind="ExternalInput").ap()
    outc = nc.dram_tensor("outc", [B, D, TW], F32, kind="ExternalOutput").ap()

    ex1i = nc.dram_tensor("ex1i", [P, TQ], BF16).ap()
    ex1o = nc.dram_tensor("ex1o", [NC, P, TQ], BF16).ap()
    ex2i = [nc.dram_tensor(f"ex2i{s}", [NC, P, B, CT], BF16).ap() for s in range(NSEG)]
    ex2o = [nc.dram_tensor(f"ex2o{s}", [NC, P, B, CT], BF16).ap() for s in range(NSEG)]
    wg_bf = nc.dram_tensor("wg_bf", [D, HID], BF16).ap()
    wu_bf = nc.dram_tensor("wu_bf", [D, HID], BF16).ap()

    stP = contextlib.ExitStack()
    stA = contextlib.ExitStack()
    stBC = contextlib.ExitStack()

    def sbuf(stk, name, shape, dt):
        return stk.enter_context(nc.sbuf_tensor(name, shape, dt))

    def psum(stk, name, shape):
        return stk.enter_context(nc.psum_tensor(name, shape, F32))

    cst1 = sbuf(stP, "cst1", [P, P], BF16)
    wcat_sb = sbuf(stP, "wcat_sb", [P, 8 * 96], BF16)
    wdt_sb = sbuf(stP, "wdt_sb", [RR, P], BF16)
    bdt_sb = sbuf(stP, "bdt_sb", [P, 1], F32)
    aneg_sb = sbuf(stP, "aneg_sb", [P, NST], F32)
    atmp_sb = sbuf(stP, "atmp_sb", [P, NST], F32)
    dp_sb = sbuf(stP, "dp_sb", [P, 1], F32)
    lng_sb = sbuf(stP, "lng_sb", [P, 8], F32)
    lnb_sb = sbuf(stP, "lnb_sb", [P, 8], F32)
    kc_sb = sbuf(stP, "kc_sb", [P, 3], F32)

    sem_names = ("sPRM", "sDT", "sMR", "sGL", "sBM", "sCM", "sEX", "sCC", "sST",
                 "sYC", "sXC", "sPE", "sDV", "sSC", "sGP", "sWG", "sWU", "sWO",
                 "sWB", "sWD")
    SM = {k: Sem(stP.enter_context(nc.semaphore(k))) for k in sem_names}
    (sPRM, sDT, sMR, sGL, sBM, sCM, sEX, sCC, sST, sYC, sXC,
     sPE, sDV, sSC, sGP, sWG, sWU, sWO, sWB, sWD) = (SM[k] for k in sem_names)

    prog = {k: [] for k in ("sync", "scalar", "vector", "gpsimd", "tensor")}

    def emit(eng, fn):
        prog[eng].append(fn)

    def wait(eng, sem, val):
        if val > 0:
            emit(eng, lambda q, s=sem.h, v=val: q.wait_ge(s, v))

    xa_sb = sbuf(stA, "xa_sb", [P, 8 * TQ], BF16)
    x2_sb = sbuf(stA, "x2_sb", [P, 8 * TQ], BF16)
    ex1_src = sbuf(stA, "ex1_src", [P, TQ], BF16)
    mur_a = sbuf(stA, "mur_a", [P, TQ], F32)
    e2r_a = sbuf(stA, "e2r_a", [P, TQ], F32)
    var_a = sbuf(stA, "var_a", [P, TQ], F32)
    rstd_a = sbuf(stA, "rstd_a", [P, TQ], F32)
    mu_b = sbuf(stA, "mu_b", [P, TQ], BF16)
    rstd_b = sbuf(stA, "rstd_b", [P, TQ], BF16)

    emit("vector", lambda q: q.memset(cst1[:], 1.0 / 1024.0).then_inc(sDV.h, 1))
    sDV.inc()
    emit("vector", lambda q: q.memset(ex1_src[96:128, :], 0.0).then_inc(sDV.h, 1))
    sDV.inc()
    emit("vector", lambda q: q.memset(kc_sb[:, 0:1], EPS).then_inc(sDV.h, 1))
    sDV.inc()
    emit("vector", lambda q: q.memset(kc_sb[:, 1:2], -0.5).then_inc(sDV.h, 1))
    sDV.inc()
    emit("vector", lambda q: q.memset(kc_sb[:, 2:3], -LN127).then_inc(sDV.h, 1))
    cINIT = sDV.inc()

    emit("gpsimd", lambda q: q.dma_start(
        out=xa_sb[:].rearrange("p (dh t) -> p dh t", dh=8),
        in_=xa.rearrange("(dh p) t -> p dh t", p=P)).then_inc(sGL.h, 16))
    sGL.dma()
    emit("gpsimd", lambda q: q.dma_start(
        out=wcat_sb[:].rearrange("p (dh r) -> p dh r", dh=8),
        in_=wcat.rearrange("(dh p) r -> p dh r", p=P)).then_inc(sGL.h, 16))
    sGL.dma()
    emit("gpsimd", lambda q: q.dma_start(out=wdt_sb[:], in_=wdt).then_inc(sGL.h, 16))
    gA = sGL.dma()
    emit("gpsimd", lambda q: q.dma_start(out=wg_bf, in_=wg).then_inc(sWB.h, 16))
    sWB.dma()
    emit("gpsimd", lambda q: q.dma_start(out=wu_bf, in_=wu).then_inc(sWB.h, 16))
    gWB = sWB.dma()

    for t_sb, t_in in ((bdt_sb, bdt), (atmp_sb, alog), (dp_sb, dpv),
                       (lng_sb, lng), (lnb_sb, lnb)):
        emit("sync", lambda q, o=t_sb, i=t_in: q.dma_start(out=o[:], in_=i).then_inc(sPRM.h, 16))
        sPRM.dma()
    cPRM = sPRM.n

    wait("scalar", sPRM, cPRM)
    emit("scalar", lambda q: q.activation(atmp_sb[:], atmp_sb[:], AF.Exp).then_inc(sSC.h, 1))
    cAEXP = sSC.inc()
    wait("scalar", sGL, gA)
    emit("scalar", lambda q: q.activation(x2_sb[:], xa_sb[:], AF.Square).then_inc(sSC.h, 1))
    cSQ = sSC.inc()
    wait("vector", sSC, cAEXP)
    emit("vector", lambda q: q.tensor_scalar_mul(aneg_sb[:], atmp_sb[:], -1.0).then_inc(sDV.h, 1))
    cANEG = sDV.inc()

    stPS1 = contextlib.ExitStack()
    pmu = psum(stPS1, "pmu", [P, TQ])
    pe2 = psum(stPS1, "pe2", [P, TQ])
    wait("tensor", sGL, gA)
    wait("tensor", sDV, cINIT)
    for dh in range(8):
        for tc in range(4):
            def mm_mu(q, dh=dh, tc=tc):
                ins = q.matmul(pmu[:, tc * 512:(tc + 1) * 512], cst1[:],
                               xa_sb[:, dh * TQ + tc * 512: dh * TQ + (tc + 1) * 512],
                               start=(dh == 0), stop=(dh == 7))
                if dh == 7 and tc == 3:
                    ins.then_inc(sPE.h, 1)
            emit("tensor", mm_mu)
    cMU = sPE.inc()
    wait("tensor", sSC, cSQ)
    for dh in range(8):
        for tc in range(4):
            def mm_e2(q, dh=dh, tc=tc):
                ins = q.matmul(pe2[:, tc * 512:(tc + 1) * 512], cst1[:],
                               x2_sb[:, dh * TQ + tc * 512: dh * TQ + (tc + 1) * 512],
                               start=(dh == 0), stop=(dh == 7))
                if dh == 7 and tc == 3:
                    ins.then_inc(sPE.h, 1)
            emit("tensor", mm_e2)
    cE2 = sPE.inc()

    wait("vector", sPE, cMU)
    emit("vector", lambda q: q.tensor_copy(mur_a[:], pmu[:]).then_inc(sDV.h, 1))
    sDV.inc()
    wait("vector", sPE, cE2)
    emit("vector", lambda q: q.tensor_copy(e2r_a[:], pe2[:]).then_inc(sDV.h, 1))
    sDV.inc()
    emit("vector", lambda q: q.tensor_mul(var_a[:], mur_a[:], mur_a[:]).then_inc(sDV.h, 1))
    sDV.inc()
    emit("vector", lambda q: q.tensor_sub(var_a[:], e2r_a[:], var_a[:]).then_inc(sDV.h, 1))
    cVAR = sDV.inc()
    wait("scalar", sDV, cVAR)
    emit("scalar", lambda q: q.activation(rstd_a[:], var_a[:], AF.Ln, bias=kc_sb[:, 0:1]).then_inc(sSC.h, 1))
    sSC.inc()
    emit("scalar", lambda q: q.activation(rstd_a[:], rstd_a[:], AF.Exp, scale=kc_sb[:, 1:2]).then_inc(sSC.h, 1))
    cRSTD = sSC.inc()
    wait("vector", sSC, cRSTD)
    emit("vector", lambda q: q.tensor_copy(mu_b[:], mur_a[:]).then_inc(sDV.h, 1))
    sDV.inc()
    emit("vector", lambda q: q.tensor_copy(rstd_b[:], rstd_a[:]).then_inc(sDV.h, 1))
    sDV.inc()
    mu_bc = bass.AP(mu_b.ap().tensor, 0, [[TQ, P], [0, 8], [1, TQ]])
    rstd_bc = bass.AP(rstd_b.ap().tensor, 0, [[TQ, P], [0, 8], [1, TQ]])
    xa_v = xa_sb[:].rearrange("p (dh t) -> p dh t", dh=8)
    x2_v = x2_sb[:].rearrange("p (dh t) -> p dh t", dh=8)
    emit("vector", lambda q: q.tensor_tensor(x2_v, xa_v, mu_bc, ALU.subtract).then_inc(sDV.h, 1))
    sDV.inc()
    emit("vector", lambda q: q.tensor_tensor(xa_v, x2_v, rstd_bc, ALU.mult).then_inc(sDV.h, 1))
    cLN0 = sDV.inc()

    stPS1.close()
    pcat = psum(stA, "pcat", [96, TQ])
    wait("tensor", sDV, cLN0)
    for dh in range(8):
        for tc in range(4):
            def mm_cat(q, dh=dh, tc=tc):
                ins = q.matmul(pcat[:, tc * 512:(tc + 1) * 512],
                               wcat_sb[:, dh * 96:(dh + 1) * 96],
                               xa_sb[:, dh * TQ + tc * 512: dh * TQ + (tc + 1) * 512],
                               start=(dh == 0), stop=(dh == 7))
                if dh == 7 and tc == 3:
                    ins.then_inc(sPE.h, 1)
            emit("tensor", mm_cat)
    cCAT = sPE.inc()
    wait("vector", sPE, cCAT)
    emit("vector", lambda q: q.tensor_copy(ex1_src[0:96, :], pcat[:]).then_inc(sDV.h, 1))
    cEX1 = sDV.inc()
    wait("sync", sDV, cEX1)
    emit("sync", lambda q: q.dma_start(out=ex1_src[96:97, :], in_=mu_b[0:1, :]).then_inc(sMR.h, 16))
    sMR.dma()
    emit("sync", lambda q: q.dma_start(out=ex1_src[97:98, :], in_=rstd_b[0:1, :]).then_inc(sMR.h, 16))
    cROW = sMR.dma()
    wait("sync", sMR, cROW)
    emit("sync", lambda q: q.dma_start(out=ex1i, in_=ex1_src[:]).then_inc(sEX.h, 16))
    cEX1D = sEX.dma()
    wait("gpsimd", sEX, cEX1D)
    emit("gpsimd", lambda q: q.collective_compute(
        "AllGather", ALU.bypass, replica_groups=[list(range(NC))],
        ins=[ex1i], outs=[ex1o]).then_inc(sCC.h, 1))
    cAG = sCC.inc()

    stA.close()

    dtb_sb = sbuf(stBC, "dtb_sb", [RR, SGT], BF16)
    delta_sb = sbuf(stBC, "delta_sb", [P, SGT], BF16)
    w_sb = sbuf(stBC, "w_sb", [P, SGT], BF16)
    yacc_sb = sbuf(stBC, "yacc_sb", [P, B * SGT], BF16)
    a_buf = sbuf(stBC, "a_buf", [P, SGT], BF16)
    u_buf = [sbuf(stBC, "u_buf0", [P, SGT], BF16), sbuf(stBC, "u_buf1", [P, SGT], BF16)]
    h_buf = sbuf(stBC, "h_buf", [P, SGT], BF16)
    bm_rep = sbuf(stBC, "bm_rep", [P, SGT], BF16)
    cm_rep = sbuf(stBC, "cm_rep", [P, SGT], BF16)
    hcar = sbuf(stBC, "hcar", [P, B * NST], F32)
    xbh_sb = sbuf(stBC, "xbh_sb", [P, SGT], BF16)
    murB_sb = sbuf(stBC, "murB_sb", [P, SGT], BF16)
    rstdB_sb = sbuf(stBC, "rstdB_sb", [P, SGT], BF16)

    wout_sb = sbuf(stBC, "wout_sb", [P, 8 * D], BF16)
    wd_sb = sbuf(stBC, "wd_sb", [P, 24 * D], BF16)
    WBL = 3
    NBL = 24 // WBL
    wg_blk = [sbuf(stBC, "wg_blk0", [P, 8 * WBL * P], BF16),
              sbuf(stBC, "wg_blk1", [P, 8 * WBL * P], BF16)]
    wu_blk = [sbuf(stBC, "wu_blk0", [P, 8 * WBL * P], BF16),
              sbuf(stBC, "wu_blk1", [P, 8 * WBL * P], BF16)]
    y_ch = sbuf(stBC, "y_ch", [P, 8 * CT], BF16)
    y2_ch = sbuf(stBC, "y2_ch", [P, 8 * CT], BF16)
    xout_ch = sbuf(stBC, "xout_ch", [P, 8 * CT], F32)
    x1_ch = sbuf(stBC, "x1_ch", [P, 8 * CT], BF16)
    ln1_ch = sbuf(stBC, "ln1_ch", [P, 8 * CT], BF16)
    h_all = sbuf(stBC, "h_all", [P, 24 * CT], BF16)
    graw_sb = sbuf(stBC, "graw_sb", [P, 24 * CT], BF16)
    muf_c = sbuf(stBC, "muf_c", [P, CT], F32)
    e2f_c = sbuf(stBC, "e2f_c", [P, CT], F32)
    varf_c = sbuf(stBC, "varf_c", [P, CT], F32)
    mub_c = sbuf(stBC, "mub_c", [P, CT], BF16)
    rstdb_c = sbuf(stBC, "rstdb_c", [P, CT], BF16)

    pz = psum(stBC, "pz", [P, 512])
    pstat = psum(stBC, "pstat", [P, 512])
    pmm = psum(stBC, "pmm", [P, 8 * CT])
    pgu = [psum(stBC, "pgu0", [P, 512]), psum(stBC, "pgu1", [P, 512])]

    wait("gpsimd", sCC, cAG)
    emit("gpsimd", lambda q: q.dma_start(
        out=wout_sb[:].rearrange("p (dh e) -> p dh e", dh=8),
        in_=wout.rearrange("(dh p) e -> p dh e", p=P)).then_inc(sWO.h, 16))
    gWOUT = sWO.dma()
    for hh in range(24):
        emit("gpsimd", lambda q, hh=hh: q.dma_start(
            out=wd_sb[:, hh * D:(hh + 1) * D],
            in_=wd.rearrange("(hh p) d -> p hh d", p=P)[:, hh, :]).then_inc(sWD.h, 16))
        sWD.dma()
    gWD = sWD.n

    scan_done = {}
    ch_done = {}
    u_done = {}
    scan_a_cnt = {}
    chunk_state = {}
    prep_done = {}
    prev_iter_last = [None]
    prev_prep = [None]
    sp_last = [0]
    zpre_last = [0]
    ex2_dma = {}
    cprev = {"ych_used": 0, "x_used": 0, "outdma": 0, "pstat_free": 0,
             "y2_free": 0, "pmm_free": 0, "x1_free": 0, "ln1_free": 0,
             "hall_free": 0, "hall_gp": 0, "mu_free": 0, "silu": {}, "upev": {}, "hm": {},
             "cgu": {}, "wgld": {}, "wuld": {}}

    def emit_bm(i, first=False):
        si, bn = divmod(i, B * NST)
        b, n = divmod(bn, NST)
        if not first:
            wait("sync", sGP, u_done[i - 1])
        emit("sync", lambda q, b=b, n=n, si=si: q.dma_start(
            out=bm_rep[:],
            in_=bass.AP(ex1o.tensor, (b * 4 + si) * P * TQ + (64 + n) * TQ,
                        [[0, P], [1, TQ]])).then_inc(sBM.h, 16))
        sBM.dma()

    def emit_cm(i, first=False):
        si, bn = divmod(i, B * NST)
        b, n = divmod(bn, NST)
        if not first:
            wait("sync", sDV, chm_cnt[i - 1])
        emit("sync", lambda q, b=b, n=n, si=si: q.dma_start(
            out=cm_rep[:],
            in_=bass.AP(ex1o.tensor, (b * 4 + si) * P * TQ + (80 + n) * TQ,
                        [[0, P], [1, TQ]])).then_inc(sCM.h, 16))
        sCM.dma()

    chm_cnt = {}

    def emit_prep(s, b):
        first = (s, b) == (0, 0)
        if first:
            wait("sync", sCC, cAG)
        else:
            wait("sync", sPE, zpre_last[0])
        emit("sync", lambda q, b=b, s=s: q.dma_start(
            out=dtb_sb[:],
            in_=bass.AP(ex1o.tensor, (b * 4 + s) * P * TQ, [[TQ, RR], [1, TQ]])
        ).then_inc(sDT.h, 16))
        cDTB = sDT.dma()
        if not first:
            wait("sync", sGP, prep_done[prev_prep[0]]["lnmult"])
        emit("sync", lambda q, b=b, s=s: q.dma_start(
            out=murB_sb[:],
            in_=bass.AP(ex1o.tensor, (b * 4 + s) * P * TQ + 96 * TQ,
                        [[0, P], [1, TQ]])).then_inc(sMR.h, 16))
        sMR.dma()
        emit("sync", lambda q, b=b, s=s: q.dma_start(
            out=rstdB_sb[:],
            in_=bass.AP(ex1o.tensor, (b * 4 + s) * P * TQ + 97 * TQ,
                        [[0, P], [1, TQ]])).then_inc(sMR.h, 16))
        cMR = sMR.dma()
        if not first:
            wait("gpsimd", sGP, prep_done[prev_prep[0]]["lnsub"])
        emit("gpsimd", lambda q, b=b, s=s: q.dma_start(
            out=xbh_sb[:], in_=xb[b][:, s * SGT:(s + 1) * SGT]).then_inc(sGL.h, 16))
        cXB = sGL.dma()
        wait("tensor", sDT, cDTB)
        for tc in range(4):
            if not (first and tc == 0):
                wait("tensor", sSC, sp_last[0])
            def mm_z(q, tc=tc):
                q.matmul(pz[:], wdt_sb[:], dtb_sb[:, tc * 512:(tc + 1) * 512],
                         start=True, stop=True).then_inc(sPE.h, 1)
            emit("tensor", mm_z)
            cz = sPE.inc()
            wait("scalar", sPE, cz)
            def sp_z(q, tc=tc):
                q.activation(delta_sb[:, tc * 512:(tc + 1) * 512], pz[:],
                             AF.Exp, bias=bdt_sb[:, 0:1])
                q.activation(delta_sb[:, tc * 512:(tc + 1) * 512],
                             delta_sb[:, tc * 512:(tc + 1) * 512],
                             AF.Ln, bias=1.0).then_inc(sSC.h, 1)
            emit("scalar", sp_z)
            sp_last[0] = sSC.inc()
        zpre_last[0] = sPE.n
        cSP = sSC.n
        wait("gpsimd", sGL, cXB)
        wait("gpsimd", sMR, cMR)
        emit("gpsimd", lambda q: q.tensor_sub(murB_sb[:], xbh_sb[:], murB_sb[:]).then_inc(sGP.h, 1))
        lnsub = sGP.inc()
        emit("gpsimd", lambda q: q.tensor_mul(xbh_sb[:], murB_sb[:], rstdB_sb[:]).then_inc(sGP.h, 1))
        lnmult = sGP.inc()
        wait("gpsimd", sSC, cSP)
        emit("gpsimd", lambda q: q.tensor_mul(w_sb[:], delta_sb[:], xbh_sb[:]).then_inc(sGP.h, 1))
        cW = sGP.inc()
        if s > 0 and b == 0:
            wait("gpsimd", sEX, ex2_dma[s - 1])
        emit("gpsimd", lambda q, b=b: q.tensor_scalar(
            yacc_sb[:, b * SGT:(b + 1) * SGT], xbh_sb[:], dp_sb[:, 0:1], None,
            ALU.mult).then_inc(sGP.h, 1))
        cY0 = sGP.inc()
        prep_done[(s, b)] = {"w": cW, "y0": cY0, "lnsub": lnsub, "lnmult": lnmult}
        prev_prep[0] = (s, b)

    def emit_scan_iter(i):
        si, bn = divmod(i, B * NST)
        b, n = divmod(bn, NST)
        up = i % 2
        if i == 0:
            wait("scalar", sDV, cANEG)
        else:
            wait("scalar", sDV, scan_done[i - 1])
        def a_exp(q, n=n):
            q.activation(a_buf[:], delta_sb[:], AF.Exp,
                         scale=aneg_sb[:, n:n + 1]).then_inc(sSC.h, 1)
        emit("scalar", a_exp)
        scan_a_cnt[i] = sSC.inc()
        wait("gpsimd", sBM, sBM.n)
        if i >= 2:
            wait("gpsimd", sDV, ch_done[i - 2])
        emit("gpsimd", lambda q, up=up: q.tensor_mul(
            u_buf[up][:], w_sb[:], bm_rep[:]).then_inc(sGP.h, 1))
        u_done[i] = sGP.inc()
        if (i + 1) < NSEG * B * NST and (i + 1) % NST != 0:
            emit_bm(i + 1)
        wait("vector", sSC, scan_a_cnt[i])
        wait("vector", sGP, u_done[i])
        if n == 0:
            wait("vector", sGP, prep_done[(si, b)]["y0"])
        if si == 0:
            emit("vector", lambda q, up=up: q.tensor_tensor_scan(
                h_buf[:], a_buf[:], u_buf[up][:], 0.0,
                ALU.mult, ALU.add).then_inc(sDV.h, 1))
        else:
            emit("vector", lambda q, up=up, b=b, n=n: q.tensor_tensor_scan(
                h_buf[:], a_buf[:], u_buf[up][:],
                hcar[:, b * NST + n: b * NST + n + 1],
                ALU.mult, ALU.add).then_inc(sDV.h, 1))
        scan_done[i] = sDV.inc()
        wait("vector", sDV, scan_done[i])
        if si < NSEG - 1:
            emit("vector", lambda q, b=b, n=n: q.tensor_copy(
                hcar[:, b * NST + n: b * NST + n + 1],
                h_buf[:, SGT - 1:SGT]).then_inc(sDV.h, 1))
            sDV.inc()
        wait("vector", sCM, sCM.n)
        emit("vector", lambda q, up=up: q.tensor_mul(
            u_buf[up][:], h_buf[:], cm_rep[:]).then_inc(sDV.h, 1))
        chm_cnt[i] = sDV.inc()
        if (i + 1) < NSEG * B * NST and (i + 1) % NST != 0:
            emit_cm(i + 1)
        emit("vector", lambda q, b=b, up=up: q.tensor_add(
            yacc_sb[:, b * SGT:(b + 1) * SGT], yacc_sb[:, b * SGT:(b + 1) * SGT],
            u_buf[up][:]).then_inc(sDV.h, 1))
        ch_done[i] = sDV.inc()
        prev_iter_last[0] = i

    def emit_ex2(s):
        wait("sync", sDV, ch_done[(s * B + 1) * NST + NST - 1])
        emit("sync", lambda q, s=s: q.dma_start(
            out=bass.AP(ex2i[s].tensor, 0, [[B * CT, P], [CT, B], [P * B * CT, NC], [1, CT]]),
            in_=yacc_sb[:].rearrange("p (b r w) -> p b r w", b=B, r=NC),
        ).then_inc(sEX.h, 16))
        ex2_dma[s] = sEX.dma()
        wait("gpsimd", sEX, ex2_dma[s])
        def a2a(q, s=s):
            with allow_ici_all_to_all():
                q.collective_compute(
                    "AllToAll", ALU.bypass, replica_groups=[list(range(NC))],
                    ins=[ex2i[s]], outs=[ex2o[s]]).then_inc(sCC.h, 1)
        emit("gpsimd", a2a)
        sCC.inc()

    y_chv = y_ch[:].rearrange("p (dh w) -> p dh w", dh=8)
    y2_chv = y2_ch[:].rearrange("p (dh w) -> p dh w", dh=8)
    x1_chv = x1_ch[:].rearrange("p (dh w) -> p dh w", dh=8)
    ln1_chv = ln1_ch[:].rearrange("p (dh w) -> p dh w", dh=8)
    mub_bc = bass.AP(mub_c.ap().tensor, 0, [[CT, P], [0, 8], [1, CT]])
    rstdb_bc = bass.AP(rstdb_c.ap().tensor, 0, [[CT, P], [0, 8], [1, CT]])

    def emit_chunk_head(s, b):
        ci = s * B + b
        wait("sync", sCC, 2 + s)
        if ci > 0:
            wait("sync", sGP, cprev["ych_used"])
        def dma_y(q, s=s, b=b):
            q.dma_start(out=y_ch[:],
                        in_=bass.AP(ex2o[s].tensor, b * CT,
                                    [[B * CT, P], [P * B * CT, 8], [1, CT]])
                        ).then_inc(sYC.h, 16)
        emit("sync", dma_y)
        cYC = sYC.dma()
        if ci > 0:
            wait("sync", sDV, cprev["x_used"])
            wait("sync", sST, cprev["outdma"])
        def dma_x(q, s=s, b=b):
            q.dma_start(out=xout_ch[:],
                        in_=bass.AP(xc.tensor, b * D * TW + s * CT,
                                    [[TW, P], [P * TW, 8], [1, CT]])
                        ).then_inc(sXC.h, 16)
        emit("sync", dma_x)
        cXC = sXC.dma()
        wait("scalar", sYC, cYC)
        if ci > 0:
            wait("scalar", sPE, cprev["y2_free"])
        emit("scalar", lambda q: q.activation(y2_ch[:], y_ch[:], AF.Square).then_inc(sSC.h, 1))
        cYSQ = sSC.inc()
        wait("tensor", sYC, cYC)
        if ci > 0:
            wait("tensor", sSC, cprev["pstat_free"])
        for dh in range(8):
            def mm_smu(q, dh=dh):
                ins = q.matmul(pstat[:, 0:CT], cst1[:], y_chv[:, dh, :],
                               start=(dh == 0), stop=(dh == 7))
                if dh == 7:
                    ins.then_inc(sPE.h, 1)
            emit("tensor", mm_smu)
        sPE.inc()
        wait("tensor", sSC, cYSQ)
        for dh in range(8):
            def mm_se2(q, dh=dh):
                ins = q.matmul(pstat[:, CT:2 * CT], cst1[:], y2_chv[:, dh, :],
                               start=(dh == 0), stop=(dh == 7))
                if dh == 7:
                    ins.then_inc(sPE.h, 1)
            emit("tensor", mm_se2)
        cSE2 = sPE.inc()
        wait("scalar", sPE, cSE2)
        if ci > 0:
            wait("scalar", sGP, cprev["mu_free"])
        emit("scalar", lambda q: q.copy(muf_c[:], pstat[:, 0:CT]).then_inc(sSC.h, 1))
        sSC.inc()
        emit("scalar", lambda q: q.copy(e2f_c[:], pstat[:, CT:2 * CT]).then_inc(sSC.h, 1))
        cEVAC = sSC.inc()
        cprev["pstat_free"] = cEVAC
        wait("gpsimd", sSC, cEVAC)
        emit("gpsimd", lambda q: q.tensor_mul(varf_c[:], muf_c[:], muf_c[:]).then_inc(sGP.h, 1))
        sGP.inc()
        emit("gpsimd", lambda q: q.tensor_sub(varf_c[:], e2f_c[:], varf_c[:]).then_inc(sGP.h, 1))
        cVARg = sGP.inc()
        wait("scalar", sGP, cVARg)
        emit("scalar", lambda q: q.activation(varf_c[:], varf_c[:], AF.Ln, bias=kc_sb[:, 0:1]).then_inc(sSC.h, 1))
        sSC.inc()
        emit("scalar", lambda q: q.activation(varf_c[:], varf_c[:], AF.Exp, scale=kc_sb[:, 1:2]).then_inc(sSC.h, 1))
        sSC.inc()
        emit("scalar", lambda q: q.copy(mub_c[:], muf_c[:]).then_inc(sSC.h, 1))
        sSC.inc()
        emit("scalar", lambda q: q.copy(rstdb_c[:], varf_c[:]).then_inc(sSC.h, 1))
        cRSTDc = sSC.inc()
        wait("gpsimd", sSC, cRSTDc)
        emit("gpsimd", lambda q: q.tensor_tensor(y2_chv, y_chv, mub_bc, ALU.subtract).then_inc(sGP.h, 1))
        cprev["ych_used"] = sGP.inc()
        emit("gpsimd", lambda q: q.tensor_tensor(y2_chv, y2_chv, rstdb_bc, ALU.mult).then_inc(sGP.h, 1))
        sGP.inc()
        gpYH = sGP.inc() if False else sGP.n
        cprev["mu_free"] = sGP.n
        wait("scalar", sGP, sGP.n)
        for dh in range(8):
            def ln_aff(q, dh=dh):
                ins = q.activation(y2_chv[:, dh, :], y2_chv[:, dh, :], AF.Identity,
                                   scale=lng_sb[:, dh:dh + 1], bias=lnb_sb[:, dh:dh + 1])
                if dh == 7:
                    ins.then_inc(sSC.h, 1)
            emit("scalar", ln_aff)
        cYH = sSC.inc()
        wait("tensor", sSC, cYH)
        wait("tensor", sWO, gWOUT)
        if ci > 0:
            wait("tensor", sDV, cprev["pmm_free"])
        for eh in range(8):
            for dh in range(8):
                def mm_att(q, eh=eh, dh=dh):
                    ins = q.matmul(pmm[:, eh * CT:(eh + 1) * CT],
                                   wout_sb[:, dh * D + eh * P: dh * D + (eh + 1) * P],
                                   y2_chv[:, dh, :],
                                   start=(dh == 0), stop=(dh == 7))
                    if eh == 7 and dh == 7:
                        ins.then_inc(sPE.h, 1)
                emit("tensor", mm_att)
        cATT = sPE.inc()
        cprev["y2_free"] = cATT
        chunk_state[(s, b)] = {"cATT": cATT, "cXC": cXC}

    def emit_chunk_x1(s, b):
        st = chunk_state[(s, b)]
        wait("vector", sPE, st["cATT"])
        wait("vector", sXC, st["cXC"])
        emit("vector", lambda q: q.tensor_add(x1_ch[:], pmm[:], xout_ch[:]).then_inc(sDV.h, 1))
        cX1 = sDV.inc()
        st["cX1"] = cX1
        cprev["x_used"] = cX1

    def emit_chunk_mid(s, b):
        ci = s * B + b
        st = chunk_state[(s, b)]
        cX1 = st["cX1"]
        wait("scalar", sDV, cX1)
        wait("scalar", sPE, st["cATT"])
        emit("scalar", lambda q: q.activation(y2_ch[:], x1_ch[:], AF.Square).then_inc(sSC.h, 1))
        cX1SQ = sSC.inc()
        wait("tensor", sDV, cX1)
        wait("tensor", sSC, cprev["pstat_free"])
        for dh in range(8):
            def mm_lmu(q, dh=dh):
                ins = q.matmul(pstat[:, 0:CT], cst1[:], x1_chv[:, dh, :],
                               start=(dh == 0), stop=(dh == 7))
                if dh == 7:
                    ins.then_inc(sPE.h, 1)
            emit("tensor", mm_lmu)
        sPE.inc()
        wait("tensor", sSC, cX1SQ)
        for dh in range(8):
            def mm_le2(q, dh=dh):
                ins = q.matmul(pstat[:, CT:2 * CT], cst1[:], y2_chv[:, dh, :],
                               start=(dh == 0), stop=(dh == 7))
                if dh == 7:
                    ins.then_inc(sPE.h, 1)
            emit("tensor", mm_le2)
        cLE2 = sPE.inc()
        cprev["y2_free"] = cLE2
        wait("scalar", sPE, cLE2)
        wait("scalar", sGP, cprev["mu_free"])
        emit("scalar", lambda q: q.copy(muf_c[:], pstat[:, 0:CT]).then_inc(sSC.h, 1))
        sSC.inc()
        emit("scalar", lambda q: q.copy(e2f_c[:], pstat[:, CT:2 * CT]).then_inc(sSC.h, 1))
        cEVAC1 = sSC.inc()
        cprev["pstat_free"] = cEVAC1
        wait("gpsimd", sSC, cEVAC1)
        emit("gpsimd", lambda q: q.tensor_mul(varf_c[:], muf_c[:], muf_c[:]).then_inc(sGP.h, 1))
        sGP.inc()
        emit("gpsimd", lambda q: q.tensor_sub(varf_c[:], e2f_c[:], varf_c[:]).then_inc(sGP.h, 1))
        cVARg1 = sGP.inc()
        wait("scalar", sGP, cVARg1)
        emit("scalar", lambda q: q.activation(varf_c[:], varf_c[:], AF.Ln, bias=kc_sb[:, 0:1]).then_inc(sSC.h, 1))
        sSC.inc()
        emit("scalar", lambda q: q.activation(varf_c[:], varf_c[:], AF.Exp,
                                              scale=kc_sb[:, 1:2], bias=kc_sb[:, 2:3]).then_inc(sSC.h, 1))
        sSC.inc()
        emit("scalar", lambda q: q.copy(mub_c[:], muf_c[:]).then_inc(sSC.h, 1))
        sSC.inc()
        emit("scalar", lambda q: q.copy(rstdb_c[:], varf_c[:]).then_inc(sSC.h, 1))
        cRSTD1 = sSC.inc()
        wait("gpsimd", sSC, cRSTD1)
        if ci > 0:
            wait("gpsimd", sPE, cprev["ln1_free"])
        emit("gpsimd", lambda q: q.tensor_tensor(ln1_chv, x1_chv, mub_bc, ALU.subtract).then_inc(sGP.h, 1))
        sGP.inc()
        emit("gpsimd", lambda q: q.tensor_tensor(ln1_chv, ln1_chv, rstdb_bc, ALU.mult).then_inc(sGP.h, 1))
        cLN1 = sGP.inc()
        cprev["mu_free"] = cLN1
        st["cLN1"] = cLN1
        emit_gu_block_load(ci, 0)
        emit_gu_block_load(ci, 1)

    def emit_gu_block_load(ci, blk):
        gb = ci * NBL + blk
        if gb < 2:
            wait("scalar", sWB, gWB)
        else:
            wait("scalar", sPE, cprev["cgu"].get(gb - 2, 0))
        def dma_wg(q, blk=blk):
            q.dma_start(out=wg_blk[blk % 2][:],
                        in_=bass.AP(wg_bf.tensor, blk * WBL * P,
                                    [[HID, P], [P * HID, 8], [1, WBL * P]])
                        ).then_inc(sWG.h, 16)
        emit("scalar", dma_wg)
        cprev["wgld"][gb] = sWG.dma()
        def dma_wu(q, blk=blk):
            q.dma_start(out=wu_blk[blk % 2][:],
                        in_=bass.AP(wu_bf.tensor, blk * WBL * P,
                                    [[HID, P], [P * HID, 8], [1, WBL * P]])
                        ).then_inc(sWU.h, 16)
        emit("scalar", dma_wu)
        cprev["wuld"][gb] = sWU.dma()

    def emit_chunk_gu(s, b, blk):
        ci = s * B + b
        st = chunk_state[(s, b)]
        gb = ci * NBL + blk
        if blk == 0:
            wait("tensor", sGP, st["cLN1"])
        wait("tensor", sWG, sWG.n)
        wait("tensor", sWU, sWU.n)
        for hq in range(WBL):
            hh = blk * WBL + hq
            par = hh % 2
            g = ci * 24 + hh
            if g >= 2:
                wait("tensor", sSC, cprev["upev"][g - 2])
            for dh in range(8):
                def mm_g(q, dh=dh, par=par, blk=blk, hq=hq):
                    q.matmul(pgu[par][:, 0:CT],
                             wg_blk[blk % 2][:, dh * WBL * P + hq * P: dh * WBL * P + (hq + 1) * P],
                             ln1_chv[:, dh, :], start=(dh == 0), stop=(dh == 7))
                emit("tensor", mm_g)
            for dh in range(8):
                def mm_u(q, dh=dh, par=par, blk=blk, hq=hq):
                    ins = q.matmul(pgu[par][:, CT:2 * CT],
                                   wu_blk[blk % 2][:, dh * WBL * P + hq * P: dh * WBL * P + (hq + 1) * P],
                                   ln1_chv[:, dh, :], start=(dh == 0), stop=(dh == 7))
                    if dh == 7:
                        ins.then_inc(sPE.h, 1)
                emit("tensor", mm_u)
            cGU = sPE.inc()
            cprev["cgu"][gb] = cGU
            wait("scalar", sPE, cGU)
            if hh == 0 and ci > 0:
                wait("scalar", sPE, cprev["hall_free"])
                wait("scalar", sGP, cprev["hall_gp"])
            def gev(q, par=par, hh=hh):
                q.copy(graw_sb[:, hh * CT:(hh + 1) * CT], pgu[par][:, 0:CT])
                q.copy(h_all[:, hh * CT:(hh + 1) * CT], pgu[par][:, CT:2 * CT]).then_inc(sSC.h, 1)
            emit("scalar", gev)
            cprev["upev"][g] = sSC.inc()
        if blk + 2 < NBL:
            emit_gu_block_load(ci, blk + 2)
        if blk == NBL - 1:
            emit("scalar", lambda q: q.activation(graw_sb[:], graw_sb[:], AF.Silu).then_inc(sSC.h, 1))
            cSIL = sSC.inc()
            wait("gpsimd", sSC, cSIL)
            emit("gpsimd", lambda q: q.tensor_mul(h_all[:], graw_sb[:], h_all[:]).then_inc(sGP.h, 1))
            st["cHALL"] = sGP.inc()
            cprev["hall_gp"] = st["cHALL"]

    def emit_chunk_down(s, b):
        ci = s * B + b
        st = chunk_state[(s, b)]
        cHALL = st["cHALL"]
        wait("tensor", sGP, cHALL)
        wait("tensor", sDV, st["cX1"])
        if ci == 0:
            wait("tensor", sWD, gWD)
        for dh in range(8):
            for hh in range(24):
                def mm_d(q, hh=hh, dh=dh):
                    ins = q.matmul(pmm[:, dh * CT:(dh + 1) * CT],
                                   wd_sb[:, hh * D + dh * P: hh * D + (dh + 1) * P],
                                   h_all[:, hh * CT:(hh + 1) * CT],
                                   start=(hh == 0), stop=(hh == 23))
                    if dh == 7 and hh == 23:
                        ins.then_inc(sPE.h, 1)
                emit("tensor", mm_d)
        cDOWN = sPE.inc()
        cprev["hall_free"] = cDOWN
        cprev["ln1_free"] = cDOWN
        st["cDOWN"] = cDOWN

    def emit_chunk_out(s, b):
        st = chunk_state[(s, b)]
        wait("vector", sPE, st["cDOWN"])
        wait("vector", sST, cprev["outdma"])
        emit("vector", lambda q: q.scalar_tensor_tensor(
            xout_ch[:], pmm[:], 1.0 / 127.0, x1_ch[:], ALU.mult, ALU.add).then_inc(sDV.h, 1))
        cOUT = sDV.inc()
        cprev["pmm_free"] = cOUT
        cprev["x1_free"] = cOUT
        wait("sync", sDV, cOUT)
        def dma_out(q, s=s, b=b):
            q.dma_start(out=bass.AP(outc.tensor, b * D * TW + s * CT,
                                    [[TW, P], [P * TW, 8], [1, CT]]),
                        in_=xout_ch[:]).then_inc(sST.h, 16)
        emit("sync", dma_out)
        cprev["outdma"] = sST.dma()

    def sched_for(s):
        sch = {}
        if s >= 2:
            sch[0] = [("down", s - 2, 1)]
            sch[1] = [("out", s - 2, 1)]
        if s >= 1:
            ps = s - 1
            sch[2] = [("head", ps, 0)]
            sch[4] = [("x1", ps, 0)]
            sch[5] = [("mid", ps, 0)]
            for k in range(8):
                sch[6 + k] = [("gu", ps, 0, k)]
            sch[14] = sch.get(14, []) + [("down", ps, 0)]
            sch[17] = [("out", ps, 0)]
            sch[18] = [("head", ps, 1)]
            sch[20] = [("x1", ps, 1)]
            sch[21] = [("mid", ps, 1)]
            for k in range(8):
                sch[22 + k] = sch.get(22 + k, []) + [("gu", ps, 1, k)]
        return sch

    def run_piece(p):
        if p[0] == "head":
            emit_chunk_head(p[1], p[2])
        elif p[0] == "x1":
            emit_chunk_x1(p[1], p[2])
        elif p[0] == "mid":
            emit_chunk_mid(p[1], p[2])
        elif p[0] == "gu":
            emit_chunk_gu(p[1], p[2], p[3])
        elif p[0] == "down":
            emit_chunk_down(p[1], p[2])
        elif p[0] == "out":
            emit_chunk_out(p[1], p[2])

    for s in range(NSEG):
        sch = sched_for(s)
        for b in range(B):
            emit_prep(s, b)
            first = (s, b) == (0, 0)
            emit_bm((s * B + b) * NST, first=first)
            emit_cm((s * B + b) * NST, first=first)
            for n in range(NST):
                i = (s * B + b) * NST + n
                emit_scan_iter(i)
                for p in sch.get(b * NST + n, []):
                    run_piece(p)
            if b == 1:
                emit_ex2(s)
    run_piece(("down", NSEG - 2, 1))
    run_piece(("out", NSEG - 2, 1))
    for cb in range(B):
        ps = NSEG - 1
        emit_chunk_head(ps, cb)
        emit_chunk_x1(ps, cb)
        emit_chunk_mid(ps, cb)
        for k in range(8):
            emit_chunk_gu(ps, cb, k)
        emit_chunk_down(ps, cb)
        emit_chunk_out(ps, cb)

    wait("sync", sST, sST.n)

    with nc.Block() as block:
        @block.sync
        def _(q):
            for fn in prog["sync"]:
                fn(q)

        @block.scalar
        def _(q):
            for fn in prog["scalar"]:
                fn(q)

        @block.vector
        def _(q):
            for fn in prog["vector"]:
                fn(q)

        @block.gpsimd
        def _(q):
            for fn in prog["gpsimd"]:
                fn(q)

        @block.tensor
        def _(q):
            for fn in prog["tensor"]:
                fn(q)

    stBC.close()
    stP.close()
    return nc


_NC_CACHE = {}


def _get_program():
    if "nc" not in _NC_CACHE:
        _NC_CACHE["nc"] = build_program()
    return _NC_CACHE["nc"]


def kernel(L0_attn_x, L0_attn_W_dtw, L0_attn_W_dt, L0_attn_b_dt, L0_attn_A_log,
           L0_attn_W_B, L0_attn_W_C, L0_attn_Dp, L0_attn_ln_g, L0_attn_ln_b,
           L0_attn_W_out, L0_ffn_w_gate, L0_ffn_w_up, L0_ffn_w_down):
    x = np.asarray(L0_attn_x, dtype=np.float32)
    x_t = np.ascontiguousarray(x.transpose(0, 2, 1))
    wcat = np.ascontiguousarray(
        np.concatenate([np.asarray(L0_attn_W_dtw).T, np.asarray(L0_attn_W_B).T,
                        np.asarray(L0_attn_W_C).T], axis=1).astype(np.float32))
    wdt_t = np.ascontiguousarray(np.asarray(L0_attn_W_dt, dtype=np.float32).T)
    bdt = np.asarray(L0_attn_b_dt, dtype=np.float32)
    alog = np.asarray(L0_attn_A_log, dtype=np.float32)
    dp = np.asarray(L0_attn_Dp, dtype=np.float32)
    lng = np.ascontiguousarray(np.asarray(L0_attn_ln_g, dtype=np.float32).reshape(8, P).T)
    lnb = np.ascontiguousarray(np.asarray(L0_attn_ln_b, dtype=np.float32).reshape(8, P).T)
    wout_t = np.ascontiguousarray(np.asarray(L0_attn_W_out, dtype=np.float32).T)
    wg_t = np.ascontiguousarray(np.asarray(L0_ffn_w_gate, dtype=np.int32).T)
    wu_t = np.ascontiguousarray(np.asarray(L0_ffn_w_up, dtype=np.int32).T)
    wd_t = np.ascontiguousarray(np.asarray(L0_ffn_w_down, dtype=np.int32).T)

    in_maps = []
    for r in range(NC):
        b, q = divmod(r, 4)
        cols = np.concatenate([np.arange(s * SGT + r * CT, s * SGT + (r + 1) * CT)
                               for s in range(NSEG)])
        in_maps.append({
            "xa": np.ascontiguousarray(x_t[b][:, q * TQ:(q + 1) * TQ]),
            "xb": np.ascontiguousarray(x_t[:, r * P:(r + 1) * P, :]),
            "xc": np.ascontiguousarray(x_t[:, :, cols]),
            "wcat": wcat,
            "wdt": np.ascontiguousarray(wdt_t[:, r * P:(r + 1) * P]),
            "bdt": np.ascontiguousarray(bdt[r * P:(r + 1) * P].reshape(P, 1)),
            "alog": np.ascontiguousarray(alog[r * P:(r + 1) * P]),
            "dpv": np.ascontiguousarray(dp[r * P:(r + 1) * P].reshape(P, 1)),
            "lng": lng,
            "lnb": lnb,
            "wout": wout_t,
            "wg": wg_t,
            "wu": wu_t,
            "wd": wd_t,
        })

    nc = _get_program()
    res = run_bass_kernel_spmd(nc, in_maps, core_ids=list(range(NC)))

    out = np.empty((B, S, D), dtype=np.float32)
    for r in range(NC):
        oc = np.asarray(res.results[r]["outc"])
        for b in range(B):
            for s in range(NSEG):
                out[b, s * SGT + r * CT: s * SGT + (r + 1) * CT, :] = \
                    oc[b][:, s * CT:(s + 1) * CT].T
    return out


# version 10
# speedup vs baseline: 1.1058x; 1.1058x over previous
import numpy as np
import contextlib
import concourse.bass as bass
import concourse.mybir as mybir
from concourse.bass_utils import run_bass_kernel_spmd
from concourse.aot_env import allow_ici_all_to_all

F32 = mybir.dt.float32
BF16 = mybir.dt.bfloat16
I32 = mybir.dt.int32
AF = mybir.ActivationFunctionType
ALU = mybir.AluOpType

B = 2
S = 8192
D = 1024
NST = 16
RR = 64
HID = 3072
P = 128
TQ = 2048
TW = 1024
NC = 8
NSEG = 4
SGT = 2048
CT = 256
EPS = 1e-5
LN127 = float(np.log(127.0))


class Sem:
    def __init__(self, h):
        self.h = h
        self.n = 0

    def dma(self):
        self.n += 16
        return self.n

    def inc(self):
        self.n += 1
        return self.n


def build_program():
    nc = bass.Bass("TRN2", target_bir_lowering=False, detect_race_conditions=False)

    xa = nc.dram_tensor("xa", [D, TQ], F32, kind="ExternalInput").ap()
    xb = nc.dram_tensor("xb", [B, P, S], F32, kind="ExternalInput").ap()
    xc = nc.dram_tensor("xc", [B, D, TW], F32, kind="ExternalInput").ap()
    wcat = nc.dram_tensor("wcat", [D, 96], F32, kind="ExternalInput").ap()
    wdt = nc.dram_tensor("wdt", [RR, P], F32, kind="ExternalInput").ap()
    bdt = nc.dram_tensor("bdt", [P, 1], F32, kind="ExternalInput").ap()
    alog = nc.dram_tensor("alog", [P, NST], F32, kind="ExternalInput").ap()
    dpv = nc.dram_tensor("dpv", [P, 1], F32, kind="ExternalInput").ap()
    lng = nc.dram_tensor("lng", [P, 8], F32, kind="ExternalInput").ap()
    lnb = nc.dram_tensor("lnb", [P, 8], F32, kind="ExternalInput").ap()
    wout = nc.dram_tensor("wout", [D, D], F32, kind="ExternalInput").ap()
    wg = nc.dram_tensor("wg", [D, HID], I32, kind="ExternalInput").ap()
    wu = nc.dram_tensor("wu", [D, HID], I32, kind="ExternalInput").ap()
    wd = nc.dram_tensor("wd", [HID, D], I32, kind="ExternalInput").ap()
    outc = nc.dram_tensor("outc", [B, D, TW], F32, kind="ExternalOutput").ap()

    ex1i = nc.dram_tensor("ex1i", [P, TQ], BF16).ap()
    ex1o = nc.dram_tensor("ex1o", [NC, P, TQ], BF16).ap()
    ex2i = [nc.dram_tensor(f"ex2i{s}", [NC, P, B, CT], BF16).ap() for s in range(NSEG)]
    ex2o = [nc.dram_tensor(f"ex2o{s}", [NC, P, B, CT], BF16).ap() for s in range(NSEG)]
    wg_bf = nc.dram_tensor("wg_bf", [D, HID], BF16).ap()
    wu_bf = nc.dram_tensor("wu_bf", [D, HID], BF16).ap()

    stP = contextlib.ExitStack()
    stA = contextlib.ExitStack()
    stBC = contextlib.ExitStack()

    def sbuf(stk, name, shape, dt):
        return stk.enter_context(nc.sbuf_tensor(name, shape, dt))

    def psum(stk, name, shape):
        return stk.enter_context(nc.psum_tensor(name, shape, F32))

    cst1 = sbuf(stP, "cst1", [P, P], BF16)
    wcat_sb = sbuf(stP, "wcat_sb", [P, 8 * 96], BF16)
    wdt_sb = sbuf(stP, "wdt_sb", [RR, P], BF16)
    bdt_sb = sbuf(stP, "bdt_sb", [P, 1], F32)
    aneg_sb = sbuf(stP, "aneg_sb", [P, NST], F32)
    atmp_sb = sbuf(stP, "atmp_sb", [P, NST], F32)
    dp_sb = sbuf(stP, "dp_sb", [P, 1], F32)
    lng_sb = sbuf(stP, "lng_sb", [P, 8], F32)
    lnb_sb = sbuf(stP, "lnb_sb", [P, 8], F32)
    kc_sb = sbuf(stP, "kc_sb", [P, 3], F32)

    sem_names = ("sPRM", "sDT", "sMR", "sGL", "sBM", "sCM", "sEX", "sCC", "sST",
                 "sYC", "sXC", "sPE", "sDV", "sSC", "sGP", "sWG", "sWU", "sWO",
                 "sWB", "sWD")
    SM = {k: Sem(stP.enter_context(nc.semaphore(k))) for k in sem_names}
    (sPRM, sDT, sMR, sGL, sBM, sCM, sEX, sCC, sST, sYC, sXC,
     sPE, sDV, sSC, sGP, sWG, sWU, sWO, sWB, sWD) = (SM[k] for k in sem_names)

    prog = {k: [] for k in ("sync", "scalar", "vector", "gpsimd", "tensor")}

    def emit(eng, fn):
        prog[eng].append(fn)

    def wait(eng, sem, val):
        if val > 0:
            emit(eng, lambda q, s=sem.h, v=val: q.wait_ge(s, v))

    xa_sb = sbuf(stA, "xa_sb", [P, 8 * TQ], BF16)
    x2_sb = sbuf(stA, "x2_sb", [P, 8 * TQ], BF16)
    ex1_src = sbuf(stA, "ex1_src", [P, TQ], BF16)
    mur_a = sbuf(stA, "mur_a", [P, TQ], F32)
    e2r_a = sbuf(stA, "e2r_a", [P, TQ], F32)
    var_a = sbuf(stA, "var_a", [P, TQ], F32)
    rstd_a = sbuf(stA, "rstd_a", [P, TQ], F32)
    mu_b = sbuf(stA, "mu_b", [P, TQ], BF16)
    rstd_b = sbuf(stA, "rstd_b", [P, TQ], BF16)

    emit("vector", lambda q: q.memset(cst1[:], 1.0 / 1024.0).then_inc(sDV.h, 1))
    sDV.inc()
    emit("vector", lambda q: q.memset(ex1_src[96:128, :], 0.0).then_inc(sDV.h, 1))
    sDV.inc()
    emit("vector", lambda q: q.memset(kc_sb[:, 0:1], EPS).then_inc(sDV.h, 1))
    sDV.inc()
    emit("vector", lambda q: q.memset(kc_sb[:, 1:2], -0.5).then_inc(sDV.h, 1))
    sDV.inc()
    emit("vector", lambda q: q.memset(kc_sb[:, 2:3], -LN127).then_inc(sDV.h, 1))
    cINIT = sDV.inc()

    emit("gpsimd", lambda q: q.dma_start(
        out=xa_sb[:].rearrange("p (dh t) -> p dh t", dh=8),
        in_=xa.rearrange("(dh p) t -> p dh t", p=P)).then_inc(sGL.h, 16))
    sGL.dma()
    emit("gpsimd", lambda q: q.dma_start(
        out=wcat_sb[:].rearrange("p (dh r) -> p dh r", dh=8),
        in_=wcat.rearrange("(dh p) r -> p dh r", p=P)).then_inc(sGL.h, 16))
    sGL.dma()
    emit("gpsimd", lambda q: q.dma_start(out=wdt_sb[:], in_=wdt).then_inc(sGL.h, 16))
    gA = sGL.dma()
    emit("gpsimd", lambda q: q.dma_start(out=wg_bf, in_=wg).then_inc(sWB.h, 16))
    sWB.dma()
    emit("gpsimd", lambda q: q.dma_start(out=wu_bf, in_=wu).then_inc(sWB.h, 16))
    gWB = sWB.dma()

    for t_sb, t_in in ((bdt_sb, bdt), (atmp_sb, alog), (dp_sb, dpv),
                       (lng_sb, lng), (lnb_sb, lnb)):
        emit("sync", lambda q, o=t_sb, i=t_in: q.dma_start(out=o[:], in_=i).then_inc(sPRM.h, 16))
        sPRM.dma()
    cPRM = sPRM.n

    wait("scalar", sPRM, cPRM)
    emit("scalar", lambda q: q.activation(atmp_sb[:], atmp_sb[:], AF.Exp).then_inc(sSC.h, 1))
    cAEXP = sSC.inc()
    wait("scalar", sGL, gA)
    emit("scalar", lambda q: q.activation(x2_sb[:], xa_sb[:], AF.Square).then_inc(sSC.h, 1))
    cSQ = sSC.inc()
    wait("vector", sSC, cAEXP)
    emit("vector", lambda q: q.tensor_scalar_mul(aneg_sb[:], atmp_sb[:], -1.0).then_inc(sDV.h, 1))
    cANEG = sDV.inc()

    stPS1 = contextlib.ExitStack()
    pmu = psum(stPS1, "pmu", [P, TQ])
    pe2 = psum(stPS1, "pe2", [P, TQ])
    wait("tensor", sGL, gA)
    wait("tensor", sDV, cINIT)
    for dh in range(8):
        for tc in range(4):
            def mm_mu(q, dh=dh, tc=tc):
                ins = q.matmul(pmu[:, tc * 512:(tc + 1) * 512], cst1[:],
                               xa_sb[:, dh * TQ + tc * 512: dh * TQ + (tc + 1) * 512],
                               start=(dh == 0), stop=(dh == 7))
                if dh == 7 and tc == 3:
                    ins.then_inc(sPE.h, 1)
            emit("tensor", mm_mu)
    cMU = sPE.inc()
    wait("tensor", sSC, cSQ)
    for dh in range(8):
        for tc in range(4):
            def mm_e2(q, dh=dh, tc=tc):
                ins = q.matmul(pe2[:, tc * 512:(tc + 1) * 512], cst1[:],
                               x2_sb[:, dh * TQ + tc * 512: dh * TQ + (tc + 1) * 512],
                               start=(dh == 0), stop=(dh == 7))
                if dh == 7 and tc == 3:
                    ins.then_inc(sPE.h, 1)
            emit("tensor", mm_e2)
    cE2 = sPE.inc()

    wait("vector", sPE, cMU)
    emit("vector", lambda q: q.tensor_copy(mur_a[:], pmu[:]).then_inc(sDV.h, 1))
    sDV.inc()
    wait("vector", sPE, cE2)
    emit("vector", lambda q: q.tensor_copy(e2r_a[:], pe2[:]).then_inc(sDV.h, 1))
    sDV.inc()
    emit("vector", lambda q: q.tensor_mul(var_a[:], mur_a[:], mur_a[:]).then_inc(sDV.h, 1))
    sDV.inc()
    emit("vector", lambda q: q.tensor_sub(var_a[:], e2r_a[:], var_a[:]).then_inc(sDV.h, 1))
    cVAR = sDV.inc()
    wait("scalar", sDV, cVAR)
    emit("scalar", lambda q: q.activation(rstd_a[:], var_a[:], AF.Ln, bias=kc_sb[:, 0:1]).then_inc(sSC.h, 1))
    sSC.inc()
    emit("scalar", lambda q: q.activation(rstd_a[:], rstd_a[:], AF.Exp, scale=kc_sb[:, 1:2]).then_inc(sSC.h, 1))
    cRSTD = sSC.inc()
    wait("vector", sSC, cRSTD)
    emit("vector", lambda q: q.tensor_copy(mu_b[:], mur_a[:]).then_inc(sDV.h, 1))
    sDV.inc()
    emit("vector", lambda q: q.tensor_copy(rstd_b[:], rstd_a[:]).then_inc(sDV.h, 1))
    sDV.inc()
    mu_bc = bass.AP(mu_b.ap().tensor, 0, [[TQ, P], [0, 8], [1, TQ]])
    rstd_bc = bass.AP(rstd_b.ap().tensor, 0, [[TQ, P], [0, 8], [1, TQ]])
    xa_v = xa_sb[:].rearrange("p (dh t) -> p dh t", dh=8)
    x2_v = x2_sb[:].rearrange("p (dh t) -> p dh t", dh=8)
    emit("vector", lambda q: q.tensor_tensor(x2_v, xa_v, mu_bc, ALU.subtract).then_inc(sDV.h, 1))
    sDV.inc()
    emit("vector", lambda q: q.tensor_tensor(xa_v, x2_v, rstd_bc, ALU.mult).then_inc(sDV.h, 1))
    cLN0 = sDV.inc()

    stPS1.close()
    pcat = psum(stA, "pcat", [96, TQ])
    wait("tensor", sDV, cLN0)
    for dh in range(8):
        for tc in range(4):
            def mm_cat(q, dh=dh, tc=tc):
                ins = q.matmul(pcat[:, tc * 512:(tc + 1) * 512],
                               wcat_sb[:, dh * 96:(dh + 1) * 96],
                               xa_sb[:, dh * TQ + tc * 512: dh * TQ + (tc + 1) * 512],
                               start=(dh == 0), stop=(dh == 7))
                if dh == 7 and tc == 3:
                    ins.then_inc(sPE.h, 1)
            emit("tensor", mm_cat)
    cCAT = sPE.inc()
    wait("vector", sPE, cCAT)
    emit("vector", lambda q: q.tensor_copy(ex1_src[0:96, :], pcat[:]).then_inc(sDV.h, 1))
    cEX1 = sDV.inc()
    wait("sync", sDV, cEX1)
    emit("sync", lambda q: q.dma_start(out=ex1_src[96:97, :], in_=mu_b[0:1, :]).then_inc(sMR.h, 16))
    sMR.dma()
    emit("sync", lambda q: q.dma_start(out=ex1_src[97:98, :], in_=rstd_b[0:1, :]).then_inc(sMR.h, 16))
    cROW = sMR.dma()
    wait("sync", sMR, cROW)
    emit("sync", lambda q: q.dma_start(out=ex1i, in_=ex1_src[:]).then_inc(sEX.h, 16))
    cEX1D = sEX.dma()
    wait("gpsimd", sEX, cEX1D)
    emit("gpsimd", lambda q: q.collective_compute(
        "AllGather", ALU.bypass, replica_groups=[list(range(NC))],
        ins=[ex1i], outs=[ex1o]).then_inc(sCC.h, 1))
    cAG = sCC.inc()

    stA.close()

    dtb_sb = sbuf(stBC, "dtb_sb", [RR, SGT], BF16)
    delta_sb = sbuf(stBC, "delta_sb", [P, SGT], BF16)
    w_sb = sbuf(stBC, "w_sb", [P, SGT], BF16)
    yacc_sb = sbuf(stBC, "yacc_sb", [P, B * SGT], BF16)
    a_buf = sbuf(stBC, "a_buf", [P, SGT], BF16)
    u_buf = [sbuf(stBC, "u_buf0", [P, SGT], BF16), sbuf(stBC, "u_buf1", [P, SGT], BF16)]
    h_buf = sbuf(stBC, "h_buf", [P, SGT], BF16)
    bm_rep = sbuf(stBC, "bm_rep", [P, SGT], BF16)
    cm_rep = sbuf(stBC, "cm_rep", [P, SGT], BF16)
    hcar = sbuf(stBC, "hcar", [P, B * NST], F32)
    xbh_sb = sbuf(stBC, "xbh_sb", [P, SGT], BF16)
    murB_sb = sbuf(stBC, "murB_sb", [P, SGT], BF16)
    rstdB_sb = sbuf(stBC, "rstdB_sb", [P, SGT], BF16)

    wout_sb = sbuf(stBC, "wout_sb", [P, 8 * D], BF16)
    wd_sb = sbuf(stBC, "wd_sb", [P, 24 * D], BF16)
    WBL = 3
    NBL = 24 // WBL
    wg_blk = [sbuf(stBC, "wg_blk0", [P, 8 * WBL * P], BF16),
              sbuf(stBC, "wg_blk1", [P, 8 * WBL * P], BF16)]
    wu_blk = [sbuf(stBC, "wu_blk0", [P, 8 * WBL * P], BF16),
              sbuf(stBC, "wu_blk1", [P, 8 * WBL * P], BF16)]
    y_ch = sbuf(stBC, "y_ch", [P, 8 * CT], BF16)
    y2_ch = sbuf(stBC, "y2_ch", [P, 8 * CT], BF16)
    xout_ch = sbuf(stBC, "xout_ch", [P, 8 * CT], F32)
    x1_ch = sbuf(stBC, "x1_ch", [P, 8 * CT], BF16)
    ln1_ch = sbuf(stBC, "ln1_ch", [P, 8 * CT], BF16)
    h_all = sbuf(stBC, "h_all", [P, 24 * CT], BF16)
    graw_sb = sbuf(stBC, "graw_sb", [P, 24 * CT], BF16)
    muf_c = sbuf(stBC, "muf_c", [P, CT], F32)
    e2f_c = sbuf(stBC, "e2f_c", [P, CT], F32)
    varf_c = sbuf(stBC, "varf_c", [P, CT], F32)
    mub_c = sbuf(stBC, "mub_c", [P, CT], BF16)
    rstdb_c = sbuf(stBC, "rstdb_c", [P, CT], BF16)

    pz = psum(stBC, "pz", [P, 512])
    pstat = psum(stBC, "pstat", [P, 512])
    pmm = psum(stBC, "pmm", [P, 8 * CT])
    pgu = [psum(stBC, "pgu0", [P, 512]), psum(stBC, "pgu1", [P, 512])]

    wait("gpsimd", sCC, cAG)
    emit("gpsimd", lambda q: q.dma_start(
        out=wout_sb[:].rearrange("p (dh e) -> p dh e", dh=8),
        in_=wout.rearrange("(dh p) e -> p dh e", p=P)).then_inc(sWO.h, 16))
    gWOUT = sWO.dma()
    for hh in range(24):
        emit("gpsimd", lambda q, hh=hh: q.dma_start(
            out=wd_sb[:, hh * D:(hh + 1) * D],
            in_=wd.rearrange("(hh p) d -> p hh d", p=P)[:, hh, :]).then_inc(sWD.h, 16))
        sWD.dma()
    gWD = sWD.n

    scan_done = {}
    ch_done = {}
    u_done = {}
    scan_a_cnt = {}
    chunk_state = {}
    prep_done = {}
    prev_iter_last = [None]
    prev_prep = [None]
    sp_last = [0]
    zpre_last = [0]
    ex2_dma = {}
    cprev = {"ych_used": 0, "x_used": 0, "outdma": 0, "pstat_free": 0,
             "y2_free": 0, "pmm_free": 0, "x1_free": 0, "ln1_free": 0,
             "hall_free": 0, "hall_gp": 0, "mu_free": 0, "silu": {}, "upev": {}, "hm": {},
             "cgu": {}, "wgld": {}, "wuld": {}}

    def emit_bm(i, first=False):
        si, bn = divmod(i, B * NST)
        b, n = divmod(bn, NST)
        if not first:
            wait("sync", sGP, u_done[i - 1])
        emit("sync", lambda q, b=b, n=n, si=si: q.dma_start(
            out=bm_rep[:],
            in_=bass.AP(ex1o.tensor, (b * 4 + si) * P * TQ + (64 + n) * TQ,
                        [[0, P], [1, TQ]])).then_inc(sBM.h, 16))
        sBM.dma()

    def emit_cm(i, first=False):
        si, bn = divmod(i, B * NST)
        b, n = divmod(bn, NST)
        if not first:
            wait("sync", sDV, chm_cnt[i - 1])
        emit("sync", lambda q, b=b, n=n, si=si: q.dma_start(
            out=cm_rep[:],
            in_=bass.AP(ex1o.tensor, (b * 4 + si) * P * TQ + (80 + n) * TQ,
                        [[0, P], [1, TQ]])).then_inc(sCM.h, 16))
        sCM.dma()

    chm_cnt = {}

    def emit_prep(s, b):
        first = (s, b) == (0, 0)
        if first:
            wait("sync", sCC, cAG)
        else:
            wait("sync", sPE, zpre_last[0])
        emit("sync", lambda q, b=b, s=s: q.dma_start(
            out=dtb_sb[:],
            in_=bass.AP(ex1o.tensor, (b * 4 + s) * P * TQ, [[TQ, RR], [1, TQ]])
        ).then_inc(sDT.h, 16))
        cDTB = sDT.dma()
        if not first:
            wait("sync", sGP, prep_done[prev_prep[0]]["lnmult"])
        emit("sync", lambda q, b=b, s=s: q.dma_start(
            out=murB_sb[:],
            in_=bass.AP(ex1o.tensor, (b * 4 + s) * P * TQ + 96 * TQ,
                        [[0, P], [1, TQ]])).then_inc(sMR.h, 16))
        sMR.dma()
        emit("sync", lambda q, b=b, s=s: q.dma_start(
            out=rstdB_sb[:],
            in_=bass.AP(ex1o.tensor, (b * 4 + s) * P * TQ + 97 * TQ,
                        [[0, P], [1, TQ]])).then_inc(sMR.h, 16))
        cMR = sMR.dma()
        if not first:
            wait("gpsimd", sDV, prep_done[prev_prep[0]]["y0"])
        emit("gpsimd", lambda q, b=b, s=s: q.dma_start(
            out=xbh_sb[:], in_=xb[b][:, s * SGT:(s + 1) * SGT]).then_inc(sGL.h, 16))
        cXB = sGL.dma()
        wait("tensor", sDT, cDTB)
        for tc in range(4):
            if not (first and tc == 0):
                wait("tensor", sSC, sp_last[0])
            def mm_z(q, tc=tc):
                q.matmul(pz[:], wdt_sb[:], dtb_sb[:, tc * 512:(tc + 1) * 512],
                         start=True, stop=True).then_inc(sPE.h, 1)
            emit("tensor", mm_z)
            cz = sPE.inc()
            wait("scalar", sPE, cz)
            def sp_z(q, tc=tc):
                q.activation(delta_sb[:, tc * 512:(tc + 1) * 512], pz[:],
                             AF.Exp, bias=bdt_sb[:, 0:1])
                q.activation(delta_sb[:, tc * 512:(tc + 1) * 512],
                             delta_sb[:, tc * 512:(tc + 1) * 512],
                             AF.Ln, bias=1.0).then_inc(sSC.h, 1)
            emit("scalar", sp_z)
            sp_last[0] = sSC.inc()
        zpre_last[0] = sPE.n
        cSP = sSC.n
        wait("gpsimd", sGL, cXB)
        wait("gpsimd", sMR, cMR)
        emit("gpsimd", lambda q: q.tensor_sub(murB_sb[:], xbh_sb[:], murB_sb[:]).then_inc(sGP.h, 1))
        lnsub = sGP.inc()
        emit("gpsimd", lambda q: q.tensor_mul(xbh_sb[:], murB_sb[:], rstdB_sb[:]).then_inc(sGP.h, 1))
        lnmult = sGP.inc()
        wait("gpsimd", sSC, cSP)
        emit("gpsimd", lambda q: q.tensor_mul(w_sb[:], delta_sb[:], xbh_sb[:]).then_inc(sGP.h, 1))
        cW = sGP.inc()
        if s > 0 and b == 0:
            wait("vector", sEX, ex2_dma[s - 1])
        wait("vector", sGP, cW)
        emit("vector", lambda q, b=b: q.tensor_scalar(
            yacc_sb[:, b * SGT:(b + 1) * SGT], xbh_sb[:], dp_sb[:, 0:1], None,
            ALU.mult).then_inc(sDV.h, 1))
        cY0 = sDV.inc()
        prep_done[(s, b)] = {"w": cW, "y0": cY0, "lnsub": lnsub, "lnmult": lnmult}
        prev_prep[0] = (s, b)

    def emit_scan_iter(i):
        si, bn = divmod(i, B * NST)
        b, n = divmod(bn, NST)
        up = i % 2
        if i == 0:
            wait("scalar", sDV, cANEG)
        else:
            wait("scalar", sDV, scan_done[i - 1])
        def a_exp(q, n=n):
            q.activation(a_buf[:], delta_sb[:], AF.Exp,
                         scale=aneg_sb[:, n:n + 1]).then_inc(sSC.h, 1)
        emit("scalar", a_exp)
        scan_a_cnt[i] = sSC.inc()
        wait("gpsimd", sBM, sBM.n)
        if i >= 2:
            wait("gpsimd", sDV, ch_done[i - 2])
        emit("gpsimd", lambda q, up=up: q.tensor_mul(
            u_buf[up][:], w_sb[:], bm_rep[:]).then_inc(sGP.h, 1))
        u_done[i] = sGP.inc()
        if (i + 1) < NSEG * B * NST and (i + 1) % NST != 0:
            emit_bm(i + 1)
        wait("vector", sSC, scan_a_cnt[i])
        wait("vector", sGP, u_done[i])
        if si == 0:
            emit("vector", lambda q, up=up: q.tensor_tensor_scan(
                h_buf[:], a_buf[:], u_buf[up][:], 0.0,
                ALU.mult, ALU.add).then_inc(sDV.h, 1))
        else:
            emit("vector", lambda q, up=up, b=b, n=n: q.tensor_tensor_scan(
                h_buf[:], a_buf[:], u_buf[up][:],
                hcar[:, b * NST + n: b * NST + n + 1],
                ALU.mult, ALU.add).then_inc(sDV.h, 1))
        scan_done[i] = sDV.inc()
        wait("vector", sDV, scan_done[i])
        if si < NSEG - 1:
            emit("vector", lambda q, b=b, n=n: q.tensor_copy(
                hcar[:, b * NST + n: b * NST + n + 1],
                h_buf[:, SGT - 1:SGT]).then_inc(sDV.h, 1))
            sDV.inc()
        wait("vector", sCM, sCM.n)
        emit("vector", lambda q, up=up: q.tensor_mul(
            u_buf[up][:], h_buf[:], cm_rep[:]).then_inc(sDV.h, 1))
        chm_cnt[i] = sDV.inc()
        if (i + 1) < NSEG * B * NST and (i + 1) % NST != 0:
            emit_cm(i + 1)
        emit("vector", lambda q, b=b, up=up: q.tensor_add(
            yacc_sb[:, b * SGT:(b + 1) * SGT], yacc_sb[:, b * SGT:(b + 1) * SGT],
            u_buf[up][:]).then_inc(sDV.h, 1))
        ch_done[i] = sDV.inc()
        prev_iter_last[0] = i

    def emit_ex2(s):
        wait("sync", sDV, ch_done[(s * B + 1) * NST + NST - 1])
        emit("sync", lambda q, s=s: q.dma_start(
            out=bass.AP(ex2i[s].tensor, 0, [[B * CT, P], [CT, B], [P * B * CT, NC], [1, CT]]),
            in_=yacc_sb[:].rearrange("p (b r w) -> p b r w", b=B, r=NC),
        ).then_inc(sEX.h, 16))
        ex2_dma[s] = sEX.dma()
        wait("gpsimd", sEX, ex2_dma[s])
        def a2a(q, s=s):
            with allow_ici_all_to_all():
                q.collective_compute(
                    "AllToAll", ALU.bypass, replica_groups=[list(range(NC))],
                    ins=[ex2i[s]], outs=[ex2o[s]]).then_inc(sCC.h, 1)
        emit("gpsimd", a2a)
        sCC.inc()

    y_chv = y_ch[:].rearrange("p (dh w) -> p dh w", dh=8)
    y2_chv = y2_ch[:].rearrange("p (dh w) -> p dh w", dh=8)
    x1_chv = x1_ch[:].rearrange("p (dh w) -> p dh w", dh=8)
    ln1_chv = ln1_ch[:].rearrange("p (dh w) -> p dh w", dh=8)
    mub_bc = bass.AP(mub_c.ap().tensor, 0, [[CT, P], [0, 8], [1, CT]])
    rstdb_bc = bass.AP(rstdb_c.ap().tensor, 0, [[CT, P], [0, 8], [1, CT]])

    def emit_chunk_head(s, b):
        ci = s * B + b
        wait("sync", sCC, 2 + s)
        if ci > 0:
            wait("sync", sGP, cprev["ych_used"])
        def dma_y(q, s=s, b=b):
            q.dma_start(out=y_ch[:],
                        in_=bass.AP(ex2o[s].tensor, b * CT,
                                    [[B * CT, P], [P * B * CT, 8], [1, CT]])
                        ).then_inc(sYC.h, 16)
        emit("sync", dma_y)
        cYC = sYC.dma()
        if ci > 0:
            wait("sync", sDV, cprev["x_used"])
            wait("sync", sST, cprev["outdma"])
        def dma_x(q, s=s, b=b):
            q.dma_start(out=xout_ch[:],
                        in_=bass.AP(xc.tensor, b * D * TW + s * CT,
                                    [[TW, P], [P * TW, 8], [1, CT]])
                        ).then_inc(sXC.h, 16)
        emit("sync", dma_x)
        cXC = sXC.dma()
        wait("scalar", sYC, cYC)
        if ci > 0:
            wait("scalar", sPE, cprev["y2_free"])
        emit("scalar", lambda q: q.activation(y2_ch[:], y_ch[:], AF.Square).then_inc(sSC.h, 1))
        cYSQ = sSC.inc()
        wait("tensor", sYC, cYC)
        if ci > 0:
            wait("tensor", sSC, cprev["pstat_free"])
        for dh in range(8):
            def mm_smu(q, dh=dh):
                ins = q.matmul(pstat[:, 0:CT], cst1[:], y_chv[:, dh, :],
                               start=(dh == 0), stop=(dh == 7))
                if dh == 7:
                    ins.then_inc(sPE.h, 1)
            emit("tensor", mm_smu)
        sPE.inc()
        wait("tensor", sSC, cYSQ)
        for dh in range(8):
            def mm_se2(q, dh=dh):
                ins = q.matmul(pstat[:, CT:2 * CT], cst1[:], y2_chv[:, dh, :],
                               start=(dh == 0), stop=(dh == 7))
                if dh == 7:
                    ins.then_inc(sPE.h, 1)
            emit("tensor", mm_se2)
        cSE2 = sPE.inc()
        wait("scalar", sPE, cSE2)
        if ci > 0:
            wait("scalar", sGP, cprev["mu_free"])
        emit("scalar", lambda q: q.copy(muf_c[:], pstat[:, 0:CT]).then_inc(sSC.h, 1))
        sSC.inc()
        emit("scalar", lambda q: q.copy(e2f_c[:], pstat[:, CT:2 * CT]).then_inc(sSC.h, 1))
        cEVAC = sSC.inc()
        cprev["pstat_free"] = cEVAC
        wait("gpsimd", sSC, cEVAC)
        emit("gpsimd", lambda q: q.tensor_mul(varf_c[:], muf_c[:], muf_c[:]).then_inc(sGP.h, 1))
        sGP.inc()
        emit("gpsimd", lambda q: q.tensor_sub(varf_c[:], e2f_c[:], varf_c[:]).then_inc(sGP.h, 1))
        cVARg = sGP.inc()
        wait("scalar", sGP, cVARg)
        emit("scalar", lambda q: q.activation(varf_c[:], varf_c[:], AF.Ln, bias=kc_sb[:, 0:1]).then_inc(sSC.h, 1))
        sSC.inc()
        emit("scalar", lambda q: q.activation(varf_c[:], varf_c[:], AF.Exp, scale=kc_sb[:, 1:2]).then_inc(sSC.h, 1))
        sSC.inc()
        emit("scalar", lambda q: q.copy(mub_c[:], muf_c[:]).then_inc(sSC.h, 1))
        sSC.inc()
        emit("scalar", lambda q: q.copy(rstdb_c[:], varf_c[:]).then_inc(sSC.h, 1))
        cRSTDc = sSC.inc()
        wait("gpsimd", sSC, cRSTDc)
        emit("gpsimd", lambda q: q.tensor_tensor(y2_chv, y_chv, mub_bc, ALU.subtract).then_inc(sGP.h, 1))
        cprev["ych_used"] = sGP.inc()
        emit("gpsimd", lambda q: q.tensor_tensor(y2_chv, y2_chv, rstdb_bc, ALU.mult).then_inc(sGP.h, 1))
        sGP.inc()
        gpYH = sGP.inc() if False else sGP.n
        cprev["mu_free"] = sGP.n
        wait("scalar", sGP, sGP.n)
        for dh in range(8):
            def ln_aff(q, dh=dh):
                ins = q.activation(y2_chv[:, dh, :], y2_chv[:, dh, :], AF.Identity,
                                   scale=lng_sb[:, dh:dh + 1], bias=lnb_sb[:, dh:dh + 1])
                if dh == 7:
                    ins.then_inc(sSC.h, 1)
            emit("scalar", ln_aff)
        cYH = sSC.inc()
        wait("tensor", sSC, cYH)
        wait("tensor", sWO, gWOUT)
        if ci > 0:
            wait("tensor", sDV, cprev["pmm_free"])
        for eh in range(8):
            for dh in range(8):
                def mm_att(q, eh=eh, dh=dh):
                    ins = q.matmul(pmm[:, eh * CT:(eh + 1) * CT],
                                   wout_sb[:, dh * D + eh * P: dh * D + (eh + 1) * P],
                                   y2_chv[:, dh, :],
                                   start=(dh == 0), stop=(dh == 7))
                    if eh == 7 and dh == 7:
                        ins.then_inc(sPE.h, 1)
                emit("tensor", mm_att)
        cATT = sPE.inc()
        cprev["y2_free"] = cATT
        chunk_state[(s, b)] = {"cATT": cATT, "cXC": cXC}

    def emit_chunk_x1(s, b):
        st = chunk_state[(s, b)]
        wait("vector", sPE, st["cATT"])
        wait("vector", sXC, st["cXC"])
        emit("vector", lambda q: q.tensor_add(x1_ch[:], pmm[:], xout_ch[:]).then_inc(sDV.h, 1))
        cX1 = sDV.inc()
        st["cX1"] = cX1
        cprev["x_used"] = cX1

    def emit_chunk_mid(s, b):
        ci = s * B + b
        st = chunk_state[(s, b)]
        cX1 = st["cX1"]
        wait("scalar", sDV, cX1)
        wait("scalar", sPE, st["cATT"])
        emit("scalar", lambda q: q.activation(y2_ch[:], x1_ch[:], AF.Square).then_inc(sSC.h, 1))
        cX1SQ = sSC.inc()
        wait("tensor", sDV, cX1)
        wait("tensor", sSC, cprev["pstat_free"])
        for dh in range(8):
            def mm_lmu(q, dh=dh):
                ins = q.matmul(pstat[:, 0:CT], cst1[:], x1_chv[:, dh, :],
                               start=(dh == 0), stop=(dh == 7))
                if dh == 7:
                    ins.then_inc(sPE.h, 1)
            emit("tensor", mm_lmu)
        sPE.inc()
        wait("tensor", sSC, cX1SQ)
        for dh in range(8):
            def mm_le2(q, dh=dh):
                ins = q.matmul(pstat[:, CT:2 * CT], cst1[:], y2_chv[:, dh, :],
                               start=(dh == 0), stop=(dh == 7))
                if dh == 7:
                    ins.then_inc(sPE.h, 1)
            emit("tensor", mm_le2)
        cLE2 = sPE.inc()
        cprev["y2_free"] = cLE2
        wait("scalar", sPE, cLE2)
        wait("scalar", sGP, cprev["mu_free"])
        emit("scalar", lambda q: q.copy(muf_c[:], pstat[:, 0:CT]).then_inc(sSC.h, 1))
        sSC.inc()
        emit("scalar", lambda q: q.copy(e2f_c[:], pstat[:, CT:2 * CT]).then_inc(sSC.h, 1))
        cEVAC1 = sSC.inc()
        cprev["pstat_free"] = cEVAC1
        wait("gpsimd", sSC, cEVAC1)
        emit("gpsimd", lambda q: q.tensor_mul(varf_c[:], muf_c[:], muf_c[:]).then_inc(sGP.h, 1))
        sGP.inc()
        emit("gpsimd", lambda q: q.tensor_sub(varf_c[:], e2f_c[:], varf_c[:]).then_inc(sGP.h, 1))
        cVARg1 = sGP.inc()
        wait("scalar", sGP, cVARg1)
        emit("scalar", lambda q: q.activation(varf_c[:], varf_c[:], AF.Ln, bias=kc_sb[:, 0:1]).then_inc(sSC.h, 1))
        sSC.inc()
        emit("scalar", lambda q: q.activation(varf_c[:], varf_c[:], AF.Exp,
                                              scale=kc_sb[:, 1:2], bias=kc_sb[:, 2:3]).then_inc(sSC.h, 1))
        sSC.inc()
        emit("scalar", lambda q: q.copy(mub_c[:], muf_c[:]).then_inc(sSC.h, 1))
        sSC.inc()
        emit("scalar", lambda q: q.copy(rstdb_c[:], varf_c[:]).then_inc(sSC.h, 1))
        cRSTD1 = sSC.inc()
        wait("gpsimd", sSC, cRSTD1)
        if ci > 0:
            wait("gpsimd", sPE, cprev["ln1_free"])
        emit("gpsimd", lambda q: q.tensor_tensor(ln1_chv, x1_chv, mub_bc, ALU.subtract).then_inc(sGP.h, 1))
        sGP.inc()
        emit("gpsimd", lambda q: q.tensor_tensor(ln1_chv, ln1_chv, rstdb_bc, ALU.mult).then_inc(sGP.h, 1))
        cLN1 = sGP.inc()
        cprev["mu_free"] = cLN1
        st["cLN1"] = cLN1
        emit_gu_block_load(ci, 0)
        emit_gu_block_load(ci, 1)

    def emit_gu_block_load(ci, blk):
        gb = ci * NBL + blk
        if gb < 2:
            wait("scalar", sWB, gWB)
        else:
            wait("scalar", sPE, cprev["cgu"].get(gb - 2, 0))
        def dma_wg(q, blk=blk):
            q.dma_start(out=wg_blk[blk % 2][:],
                        in_=bass.AP(wg_bf.tensor, blk * WBL * P,
                                    [[HID, P], [P * HID, 8], [1, WBL * P]])
                        ).then_inc(sWG.h, 16)
        emit("scalar", dma_wg)
        cprev["wgld"][gb] = sWG.dma()
        def dma_wu(q, blk=blk):
            q.dma_start(out=wu_blk[blk % 2][:],
                        in_=bass.AP(wu_bf.tensor, blk * WBL * P,
                                    [[HID, P], [P * HID, 8], [1, WBL * P]])
                        ).then_inc(sWU.h, 16)
        emit("scalar", dma_wu)
        cprev["wuld"][gb] = sWU.dma()

    def emit_chunk_gu(s, b, blk):
        ci = s * B + b
        st = chunk_state[(s, b)]
        gb = ci * NBL + blk
        if blk == 0:
            wait("tensor", sGP, st["cLN1"])
        wait("tensor", sWG, sWG.n)
        wait("tensor", sWU, sWU.n)
        for hq in range(WBL):
            hh = blk * WBL + hq
            par = hh % 2
            g = ci * 24 + hh
            if g >= 2:
                wait("tensor", sSC, cprev["upev"][g - 2])
            for dh in range(8):
                def mm_g(q, dh=dh, par=par, blk=blk, hq=hq):
                    q.matmul(pgu[par][:, 0:CT],
                             wg_blk[blk % 2][:, dh * WBL * P + hq * P: dh * WBL * P + (hq + 1) * P],
                             ln1_chv[:, dh, :], start=(dh == 0), stop=(dh == 7))
                emit("tensor", mm_g)
            for dh in range(8):
                def mm_u(q, dh=dh, par=par, blk=blk, hq=hq):
                    ins = q.matmul(pgu[par][:, CT:2 * CT],
                                   wu_blk[blk % 2][:, dh * WBL * P + hq * P: dh * WBL * P + (hq + 1) * P],
                                   ln1_chv[:, dh, :], start=(dh == 0), stop=(dh == 7))
                    if dh == 7:
                        ins.then_inc(sPE.h, 1)
                emit("tensor", mm_u)
            cGU = sPE.inc()
            cprev["cgu"][gb] = cGU
            wait("scalar", sPE, cGU)
            if hh == 0 and ci > 0:
                wait("scalar", sPE, cprev["hall_free"])
                wait("scalar", sGP, cprev["hall_gp"])
            def gev(q, par=par, hh=hh):
                q.copy(graw_sb[:, hh * CT:(hh + 1) * CT], pgu[par][:, 0:CT])
                q.copy(h_all[:, hh * CT:(hh + 1) * CT], pgu[par][:, CT:2 * CT]).then_inc(sSC.h, 1)
            emit("scalar", gev)
            cprev["upev"][g] = sSC.inc()
        if blk + 2 < NBL:
            emit_gu_block_load(ci, blk + 2)
        if blk == NBL - 1:
            emit("scalar", lambda q: q.activation(graw_sb[:], graw_sb[:], AF.Silu).then_inc(sSC.h, 1))
            cSIL = sSC.inc()
            wait("gpsimd", sSC, cSIL)
            emit("gpsimd", lambda q: q.tensor_mul(h_all[:], graw_sb[:], h_all[:]).then_inc(sGP.h, 1))
            st["cHALL"] = sGP.inc()
            cprev["hall_gp"] = st["cHALL"]

    def emit_chunk_down(s, b):
        ci = s * B + b
        st = chunk_state[(s, b)]
        cHALL = st["cHALL"]
        wait("tensor", sGP, cHALL)
        wait("tensor", sDV, st["cX1"])
        if ci == 0:
            wait("tensor", sWD, gWD)
        for dh in range(8):
            for hh in range(24):
                def mm_d(q, hh=hh, dh=dh):
                    ins = q.matmul(pmm[:, dh * CT:(dh + 1) * CT],
                                   wd_sb[:, hh * D + dh * P: hh * D + (dh + 1) * P],
                                   h_all[:, hh * CT:(hh + 1) * CT],
                                   start=(hh == 0), stop=(hh == 23))
                    if dh == 7 and hh == 23:
                        ins.then_inc(sPE.h, 1)
                emit("tensor", mm_d)
        cDOWN = sPE.inc()
        cprev["hall_free"] = cDOWN
        cprev["ln1_free"] = cDOWN
        st["cDOWN"] = cDOWN

    def emit_chunk_out(s, b):
        st = chunk_state[(s, b)]
        wait("vector", sPE, st["cDOWN"])
        wait("vector", sST, cprev["outdma"])
        emit("vector", lambda q: q.scalar_tensor_tensor(
            xout_ch[:], pmm[:], 1.0 / 127.0, x1_ch[:], ALU.mult, ALU.add).then_inc(sDV.h, 1))
        cOUT = sDV.inc()
        cprev["pmm_free"] = cOUT
        cprev["x1_free"] = cOUT
        wait("sync", sDV, cOUT)
        def dma_out(q, s=s, b=b):
            q.dma_start(out=bass.AP(outc.tensor, b * D * TW + s * CT,
                                    [[TW, P], [P * TW, 8], [1, CT]]),
                        in_=xout_ch[:]).then_inc(sST.h, 16)
        emit("sync", dma_out)
        cprev["outdma"] = sST.dma()

    def sched_for(s):
        sch = {}
        if s >= 2:
            sch[0] = [("down", s - 2, 1)]
            sch[1] = [("out", s - 2, 1)]
        if s >= 1:
            ps = s - 1
            sch[4] = [("head", ps, 0)]
            sch[6] = [("x1", ps, 0)]
            sch[7] = [("mid", ps, 0)]
            for k in range(8):
                sch[8 + k] = [("gu", ps, 0, k)]
            sch[16] = sch.get(16, []) + [("down", ps, 0)]
            sch[19] = [("out", ps, 0)]
            sch[20] = [("head", ps, 1)]
            sch[22] = [("x1", ps, 1)]
            sch[23] = [("mid", ps, 1)]
            for k in range(8):
                sch[24 + k] = sch.get(24 + k, []) + [("gu", ps, 1, k)]
        return sch

    def run_piece(p):
        if p[0] == "head":
            emit_chunk_head(p[1], p[2])
        elif p[0] == "x1":
            emit_chunk_x1(p[1], p[2])
        elif p[0] == "mid":
            emit_chunk_mid(p[1], p[2])
        elif p[0] == "gu":
            emit_chunk_gu(p[1], p[2], p[3])
        elif p[0] == "down":
            emit_chunk_down(p[1], p[2])
        elif p[0] == "out":
            emit_chunk_out(p[1], p[2])

    for s in range(NSEG):
        sch = sched_for(s)
        for b in range(B):
            emit_prep(s, b)
            first = (s, b) == (0, 0)
            emit_bm((s * B + b) * NST, first=first)
            emit_cm((s * B + b) * NST, first=first)
            for n in range(NST):
                i = (s * B + b) * NST + n
                emit_scan_iter(i)
                for p in sch.get(b * NST + n, []):
                    run_piece(p)
            if b == 1:
                emit_ex2(s)
    run_piece(("down", NSEG - 2, 1))
    run_piece(("out", NSEG - 2, 1))
    for cb in range(B):
        ps = NSEG - 1
        emit_chunk_head(ps, cb)
        emit_chunk_x1(ps, cb)
        emit_chunk_mid(ps, cb)
        for k in range(8):
            emit_chunk_gu(ps, cb, k)
        emit_chunk_down(ps, cb)
        emit_chunk_out(ps, cb)

    wait("sync", sST, sST.n)

    with nc.Block() as block:
        @block.sync
        def _(q):
            for fn in prog["sync"]:
                fn(q)

        @block.scalar
        def _(q):
            for fn in prog["scalar"]:
                fn(q)

        @block.vector
        def _(q):
            for fn in prog["vector"]:
                fn(q)

        @block.gpsimd
        def _(q):
            for fn in prog["gpsimd"]:
                fn(q)

        @block.tensor
        def _(q):
            for fn in prog["tensor"]:
                fn(q)

    stBC.close()
    stP.close()
    return nc


_NC_CACHE = {}


def _get_program():
    if "nc" not in _NC_CACHE:
        _NC_CACHE["nc"] = build_program()
    return _NC_CACHE["nc"]


def kernel(L0_attn_x, L0_attn_W_dtw, L0_attn_W_dt, L0_attn_b_dt, L0_attn_A_log,
           L0_attn_W_B, L0_attn_W_C, L0_attn_Dp, L0_attn_ln_g, L0_attn_ln_b,
           L0_attn_W_out, L0_ffn_w_gate, L0_ffn_w_up, L0_ffn_w_down):
    x = np.asarray(L0_attn_x, dtype=np.float32)
    x_t = np.ascontiguousarray(x.transpose(0, 2, 1))
    wcat = np.ascontiguousarray(
        np.concatenate([np.asarray(L0_attn_W_dtw).T, np.asarray(L0_attn_W_B).T,
                        np.asarray(L0_attn_W_C).T], axis=1).astype(np.float32))
    wdt_t = np.ascontiguousarray(np.asarray(L0_attn_W_dt, dtype=np.float32).T)
    bdt = np.asarray(L0_attn_b_dt, dtype=np.float32)
    alog = np.asarray(L0_attn_A_log, dtype=np.float32)
    dp = np.asarray(L0_attn_Dp, dtype=np.float32)
    lng = np.ascontiguousarray(np.asarray(L0_attn_ln_g, dtype=np.float32).reshape(8, P).T)
    lnb = np.ascontiguousarray(np.asarray(L0_attn_ln_b, dtype=np.float32).reshape(8, P).T)
    wout_t = np.ascontiguousarray(np.asarray(L0_attn_W_out, dtype=np.float32).T)
    wg_t = np.ascontiguousarray(np.asarray(L0_ffn_w_gate, dtype=np.int32).T)
    wu_t = np.ascontiguousarray(np.asarray(L0_ffn_w_up, dtype=np.int32).T)
    wd_t = np.ascontiguousarray(np.asarray(L0_ffn_w_down, dtype=np.int32).T)

    in_maps = []
    for r in range(NC):
        b, q = divmod(r, 4)
        cols = np.concatenate([np.arange(s * SGT + r * CT, s * SGT + (r + 1) * CT)
                               for s in range(NSEG)])
        in_maps.append({
            "xa": np.ascontiguousarray(x_t[b][:, q * TQ:(q + 1) * TQ]),
            "xb": np.ascontiguousarray(x_t[:, r * P:(r + 1) * P, :]),
            "xc": np.ascontiguousarray(x_t[:, :, cols]),
            "wcat": wcat,
            "wdt": np.ascontiguousarray(wdt_t[:, r * P:(r + 1) * P]),
            "bdt": np.ascontiguousarray(bdt[r * P:(r + 1) * P].reshape(P, 1)),
            "alog": np.ascontiguousarray(alog[r * P:(r + 1) * P]),
            "dpv": np.ascontiguousarray(dp[r * P:(r + 1) * P].reshape(P, 1)),
            "lng": lng,
            "lnb": lnb,
            "wout": wout_t,
            "wg": wg_t,
            "wu": wu_t,
            "wd": wd_t,
        })

    nc = _get_program()
    res = run_bass_kernel_spmd(nc, in_maps, core_ids=list(range(NC)))

    out = np.empty((B, S, D), dtype=np.float32)
    for r in range(NC):
        oc = np.asarray(res.results[r]["outc"])
        for b in range(B):
            for s in range(NSEG):
                out[b, s * SGT + r * CT: s * SGT + (r + 1) * CT, :] = \
                    oc[b][:, s * CT:(s + 1) * CT].T
    return out
